# version 21
# baseline (speedup 1.0000x reference)
import math
import numpy as np
import concourse.bass as bass
import concourse.mybir as mybir
from concourse.bass_utils import run_bass_kernel_spmd
from contextlib import ExitStack

F32 = mybir.dt.float32
BF16 = mybir.dt.bfloat16
AF = mybir.ActivationFunctionType
ALU = mybir.AluOpType
AX = mybir.AxisListType

S = 2048
D = 4096
H = 32
HD = 128
CH = 1024
NT = CH // 128
NE = 64
RK = 8
RD = 16
NK = D // 128
SCALE = HD ** -0.5
SLOPES = [2.0 ** (-(i + 1) / 4.0) for i in range(H)]
WIN = [int(math.ceil(140.0 / s)) for s in SLOPES]
NEG = -1.0e9


class P:

    ENG = ("sync", "gpsimd", "tensor", "vector", "scalar")

    def __init__(self):
        self.q = {e: [] for e in self.ENG}
        self.ctr = {}
        self.lastwait = {e: {} for e in self.ENG}
        self.sems = {}
        self.mmrun = 0
        self.free = {}

    def inc(self, sem, by=1):
        self.ctr[sem] = self.ctr.get(sem, 0) + by
        return self.ctr[sem]

    def cur(self, sem):
        return self.ctr.get(sem, 0)

    def wait(self, eng, sem, val):
        if val is None or val <= 0:
            return
        if sem.startswith(("ds", "dg", "do")):
            val = self.ctr.get(sem, 0)
        lw = self.lastwait[eng]
        if lw.get(sem, 0) >= val:
            return
        lw[sem] = val
        self.q[eng].append(lambda E, s=sem, v=val: E.wait_ge(self.sems[s], v))

    def op(self, eng, fn, inc=None, incby=1):
        if eng == "tensor":
            self.mmrun += 1
            if self.mmrun >= 12:
                self.mmrun = 0
                self.q[eng].append(lambda E: E.nop())
        if inc is None:
            self.q[eng].append(fn)
            return None
        v = self.inc(inc, incby)
        self.q[eng].append(lambda E, s=inc, b=incby: fn(E).then_inc(self.sems[s], b))
        return v

    def need(self, eng, key):
        if key in self.free:
            sem, val = self.free[key]
            self.wait(eng, sem, val)

    def release(self, key, sem, val):
        self.free[key] = (sem, val)


def build_program():
    nc = bass.Bass("TRN2", target_bir_lowering=False)
    p = P()

    dx = nc.dram_tensor("x", [S, D], F32, kind="ExternalInput").ap()
    dwq = nc.dram_tensor("wq", [D, D], F32, kind="ExternalInput").ap()
    dwkv = nc.dram_tensor("wkv", [D, 2 * HD], F32, kind="ExternalInput").ap()
    dwo = nc.dram_tensor("wo", [D, D], F32, kind="ExternalInput").ap()
    dawt = nc.dram_tensor("awt", [D, 512], F32, kind="ExternalInput").ap()
    dbwt = nc.dram_tensor("bwt", [512, D], F32, kind="ExternalInput").ap()
    dwqwt = nc.dram_tensor("wqwt", [D, RD], F32, kind="ExternalInput").ap()
    dkeyst = nc.dram_tensor("keyst", [RD, NE], F32, kind="ExternalInput").ap()
    dab = nc.dram_tensor("ab", [H, 128, S + 128], F32, kind="ExternalInput").ap()
    dbc = nc.dram_tensor("bc", [128, H, 18], F32, kind="ExternalInput").ap()
    dtrx = nc.dram_tensor("trx", [128, 896], F32, kind="ExternalInput").ap()
    dident = nc.dram_tensor("ident", [128, 128], F32, kind="ExternalInput").ap()
    dout = nc.dram_tensor("out", [S, D], F32, kind="ExternalOutput").ap()

    ctx = ExitStack()
    sb = lambda n, sh, dt: ctx.enter_context(nc.sbuf_tensor(n, sh, dt))
    psal = lambda n, sh, dt: ctx.enter_context(nc.psum_tensor(n, sh, dt))

    kT = sb("kT", [128, S], BF16)
    vT = sb("vT", [128, S], BF16)
    ident = sb("identsb", [128, 128], BF16)
    wqwT = sb("wqwT", [128, NK, RD], BF16)
    keysT = sb("keysT", [RD, NE], BF16)
    wstream = sb("wstream", [128, 8192], BF16)
    astream = sb("astream", [128, 8192], BF16)
    zgT = sb("zgT", [128, 4, CH], BF16)
    xnT = sb("xnT", [128, NK, CH], BF16)
    QC = sb("QC", [128, H, CH], BF16)
    scratch = sb("scratch", [128, 6784], F32)
    zgtok = scratch[:, 0:512].bitcast(BF16).rearrange("p (a b) -> p a b", b=512)
    gate = scratch[:, 512:1024]
    qrT = scratch[0:16, 1024:1536].bitcast(BF16)
    s_sb = sb("s_sb", [128, NE], F32)
    s2_sb = sb("s2_sb", [128, NE], F32)
    exps = sb("exps", [128, NE], F32)
    maskb = sb("maskb", [128, NE], F32)
    gnum = sb("gnum", [128, NE], F32)
    m8a = sb("m8a", [128, 8], F32)
    m8b = sb("m8b", [128, 8], F32)
    cols = sb("cols", [128, 48], F32)
    onesr = sb("onesr", [1, 128], BF16)
    onescol = sb("onescol", [128, 1], BF16)
    stats = sb("stats", [128, 8], F32)

    v_xstage2 = [scratch[:, 0:4096]]
    v_xnstage = astream[:, 4096:8192]
    v_sp = [scratch[:, 0:2048], scratch[:, 2048:4096]]
    v_ab = scratch[:, 4096:6272]
    v_un = [astream[:, s * 2048:(s + 1) * 2048] for s in range(4)]
    v_ut = [scratch[:, 6272:6528].bitcast(BF16), scratch[:, 6528:6784].bitcast(BF16)]
    v_trx = scratch[:, 4096:4992]
    v_bc = scratch[:, 4992:5568].rearrange("p (h m) -> p h m", m=18)
    v_zrow = [scratch[0:1, 5568:6080], scratch[0:1, 6080:6592]]
    v_xrd = [scratch[:, 4096:4608], scratch[:, 4608:5120], scratch[:, 6144:6656]]
    v_ost = [scratch[:, 5120:5632], scratch[:, 5632:6144]]

    c_nsum = cols[:, 0:1]
    c_sumsq = cols[:, 1:2]
    c_msq = cols[:, 2:3]
    c_var = cols[:, 3:4]
    c_t1 = cols[:, 4:5]
    c_rstd = cols[:, 5:6]
    c_lbias = cols[:, 6:7]
    c_negm1 = cols[:, 7:8]
    c_Z64 = cols[:, 8:9]
    c_rz64 = cols[:, 9:10]
    c_eps = cols[:, 34:35]
    c_negB = cols[:, 35:36]
    c_lnz = cols[:, 36:37]
    c_nm = [cols[:, 10 + i:11 + i] for i in range(8)]
    c_Z = [cols[:, 18 + i:19 + i] for i in range(8)]
    c_rz = [cols[:, 26 + i:27 + i] for i in range(8)]

    ps_s = [psal("ps_s0", [128, 512], F32), psal("ps_s1", [128, 512], F32)]
    ps_tf = [psal("ps_t0", [128, 512], F32), psal("ps_t1", [128, 512], F32)]
    ps_t = [t[:].bitcast(BF16) for t in ps_tf]
    ps_c = [psal("ps_c0", [128, 512], F32), psal("ps_c1", [128, 512], F32)]
    ps_a = [psal("ps_a0", [128, 512], F32), psal("ps_a1", [128, 512], F32)]

    for name in ("pe", "dv", "ac", "dgp", "dgw0", "dgw1", "dga0", "dga1",
                 "dgo0", "dgo1", "dsx", "dsab", "dsa0", "dsa1", "dsr0", "dsr1",
                 "dsr2", "do0", "do1"):
        p.sems[name] = ctx.enter_context(nc.semaphore(name))

    wq_half_view = [wstream[:, 0:4096].rearrange("p (k n) -> p k n", n=256),
                    wstream[:, 4096:8192].rearrange("p (k n) -> p k n", n=256)]
    wkv_view = wstream[:, 0:4096].rearrange("p (k n) -> p k n", n=128)
    b_view = [wstream[:, 0:2048].rearrange("p (k n) -> p k n", n=512),
              wstream[:, 4096:6144].rearrange("p (k n) -> p k n", n=512)]
    aq_view = [astream[:, 0:4096].rearrange("p (k n) -> p k n", n=512),
               astream[:, 4096:8192].rearrange("p (k n) -> p k n", n=512)]
    wo_slab = [astream[:, 0:4096].rearrange("p (k n) -> p k n", n=512),
               astream[:, 4096:8192].rearrange("p (k n) -> p k n", n=512)]

    dv_eps = p.op("vector", lambda E: E.memset(c_eps, 1.0e-5), inc="dv")
    p.wait("vector", "dv", dv_eps)
    dv_eps = p.op("vector", lambda E: E.memset(c_negB, -40.0), inc="dv")
    p.wait("vector", "dv", dv_eps)
    dv_ones = p.op("vector", lambda E: E.memset(onesr[:], 1.0), inc="dv")
    p.wait("vector", "dv", dv_ones)
    dv_ones = p.op("vector", lambda E: E.memset(onescol[:], 1.0), inc="dv")
    ld_ident = p.op("gpsimd", lambda E: E.dma_start(out=ident[:], in_=dident),
                    inc="dgp", incby=16)
    ld_wqwt = p.op("gpsimd", lambda E: E.dma_start(
        out=wqwT[:], in_=dwqwt.rearrange("(k p) r -> p k r", p=128)), inc="dgp", incby=16)
    ld_keyst = p.op("gpsimd", lambda E: E.dma_start(out=keysT[:], in_=dkeyst),
                    inc="dgp", incby=16)

    def emit_ln_tile(tok0_, t):
        row0 = tok0_ + t * 128
        xsl = 0
        v_xstage = v_xstage2[xsl]
        p.need("sync", ("xstage", xsl))
        p.need("sync", ("xstage_dv", xsl))
        v = p.op("sync", lambda E, r=row0, xs=v_xstage: E.dma_start(
            out=xs, in_=dx[r:r + 128, :]), inc="dsx", incby=16)
        p.wait("vector", "dsx", v)
        dv_red = p.op("vector", lambda E, xs=v_xstage: E.reduce_sum(
            c_nsum, xs, axis=AX.X, negate=True), inc="dv")
        p.release(("xstage_dv", xsl), "dv", dv_red)
        p.wait("scalar", "dsx", v)
        p.need("scalar", ("xnstage",))
        ac_sq = p.op("scalar", lambda E, xs=v_xstage: E.activation(
            v_xnstage, xs, AF.Square, accum_out=c_sumsq), inc="ac")
        p.wait("vector", "dv", dv_red)
        dv_msq = p.op("vector", lambda E: E.scalar_tensor_tensor(
            c_msq, c_nsum, 1.0 / (float(D) * float(D)), c_nsum, ALU.mult, ALU.mult),
            inc="dv")
        p.wait("vector", "ac", ac_sq)
        p.wait("vector", "dv", dv_msq)
        dv_var = p.op("vector", lambda E: E.scalar_tensor_tensor(
            c_var, c_sumsq, 1.0 / float(D), c_msq, ALU.mult, ALU.subtract), inc="dv")
        p.wait("scalar", "dv", dv_var)
        p.wait("scalar", "dv", dv_eps)
        ac_ln = p.op("scalar", lambda E: E.activation(c_t1, c_var, AF.Ln, bias=c_eps),
                     inc="ac")
        p.wait("scalar", "ac", ac_ln)
        ac_rstd = p.op("scalar", lambda E: E.activation(c_rstd, c_t1, AF.Exp, scale=-0.5),
                       inc="ac")
        p.wait("vector", "ac", ac_rstd)
        dv_lb = p.op("vector", lambda E: E.scalar_tensor_tensor(
            c_lbias, c_nsum, 1.0 / float(D), c_rstd, ALU.mult, ALU.mult), inc="dv")
        p.wait("scalar", "dv", dv_lb)
        ac_apply = p.op("scalar", lambda E, xs=v_xstage: E.activation(
            v_xnstage, xs, AF.Identity, bias=c_lbias, scale=c_rstd), inc="ac")
        p.release(("xstage", xsl), "ac", ac_apply)

        p.wait("tensor", "dgp", ld_ident)
        p.wait("tensor", "ac", ac_apply)
        last_pe = None
        last_dv = None
        for kb in range(8):
            bank = kb % 2
            p.need("tensor", ("ps_t", bank))
            pe_tr = None
            for j in range(4):
                k = kb * 4 + j
                pe_tr = p.op("tensor", lambda E, b=bank, jj=j, kk=k: E.transpose(
                    ps_t[b][:, jj * 128:(jj + 1) * 128],
                    v_xnstage[:, kk * 128:(kk + 1) * 128], ident[:]), inc="pe")
            p.wait("vector", "pe", pe_tr)
            p.need("vector", ("xnT_w",))
            last_dv = p.op("vector", lambda E, b=bank, kb_=kb, tt=t: E.tensor_copy(
                xnT[:, kb_ * 4:kb_ * 4 + 4, tt * 128:(tt + 1) * 128],
                ps_t[b][:, 0:512].rearrange("p (a b) -> p a b", b=128)), inc="dv")
            p.release(("ps_t", bank), "dv", last_dv)
            last_pe = pe_tr
        p.release(("xnstage",), "pe", last_pe)
        return last_dv

    ln1_dv = [None]
    for c in range(2):
        tok0 = c * CH
        if c > 0:
            for sem in ("dsab", "dsr0", "dsr1", "dsr2", "do0", "do1"):
                for eng in ("vector", "scalar", "tensor"):
                    p.wait(eng, sem, p.cur(sem))

        if c == 0:
            for t in range(NT):
                emit_ln_tile(tok0, t)
            dv_xnT_done = p.cur("dv")
        else:
            dv_xnT_done = ln1_dv[0]

        p.need("gpsimd", ("wstream_h", 0))
        g_k = p.op("gpsimd", lambda E: E.dma_start(
            out=wkv_view, in_=dwkv.rearrange("(k p) n -> p k n", p=128)[:, :, 0:HD]),
            inc="dgw0", incby=16)
        p.wait("tensor", "dgw0", g_k)
        p.wait("tensor", "dv", dv_xnT_done)
        for half in range(2):
            bank = half % 2
            p.need("tensor", ("ps_a", bank))
            pe_k = None
            for k in range(NK):
                pe_k = p.op("tensor", lambda E, b=bank, kk=k, hh=half: E.matmul(
                    ps_a[b][:], wkv_view[:, kk, :], xnT[:, kk, hh * 512:(hh + 1) * 512],
                    start=(kk == 0), stop=(kk == NK - 1)),
                    inc="pe" if k == NK - 1 else None)
            p.wait("vector", "pe", pe_k)
            dv_k = p.op("vector", lambda E, b=bank, hh=half, t0=tok0: E.tensor_copy(
                kT[:, t0 + hh * 512: t0 + (hh + 1) * 512], ps_a[b][:]), inc="dv")
            p.release(("ps_a", bank), "dv", dv_k)
        p.need("gpsimd", ("wstream_h", 0))
        p.wait("gpsimd", "pe", p.cur("pe"))
        g_v = p.op("gpsimd", lambda E: E.dma_start(
            out=wkv_view, in_=dwkv.rearrange("(k p) n -> p k n", p=128)[:, :, HD:2 * HD]),
            inc="dgw0", incby=16)
        p.wait("tensor", "dgw0", g_v)
        for t in range(NT):
            bank = t % 2
            p.need("tensor", ("ps_a", bank))
            pe_v = None
            for k in range(NK):
                pe_v = p.op("tensor", lambda E, b=bank, kk=k, tt=t: E.matmul(
                    ps_a[b][:, 0:HD], xnT[:, kk, tt * 128:(tt + 1) * 128], wkv_view[:, kk, :],
                    start=(kk == 0), stop=(kk == NK - 1)),
                    inc="pe" if k == NK - 1 else None)
            p.wait("vector", "pe", pe_v)
            dv_v = p.op("vector", lambda E, b=bank, tt=t, t0=tok0: E.tensor_copy(
                vT[:, t0 + tt * 128: t0 + (tt + 1) * 128], ps_a[b][:, 0:HD]), inc="dv")
            p.release(("ps_a", bank), "dv", dv_v)
        p.release(("wstream_h", 0), "pe", p.cur("pe"))
        p.wait("tensor", "dgp", ld_wqwt)
        for half in range(2):
            bank = half % 2
            p.need("tensor", ("ps_a", bank))
            pe_q = None
            for k in range(NK):
                pe_q = p.op("tensor", lambda E, b=bank, kk=k, hh=half: E.matmul(
                    ps_a[b][0:RD, :], wqwT[:, kk, :], xnT[:, kk, hh * 512:(hh + 1) * 512],
                    start=(kk == 0), stop=(kk == NK - 1)),
                    inc="pe" if k == NK - 1 else None)
            p.wait("vector", "pe", pe_q)
            dv_q = p.op("vector", lambda E, b=bank, hh=half: E.tensor_copy(
                qrT[:, hh * 512:(hh + 1) * 512], ps_a[b][0:RD, :]), inc="dv")
            p.release(("ps_a", bank), "dv", dv_q)
        dv_qrt = p.cur("dv")
        p.wait("tensor", "dgp", ld_keyst)
        p.wait("tensor", "dv", dv_qrt)
        for t in range(NT):
            bank = t % 2
            p.need("tensor", ("ps_a", bank))
            pe_s = p.op("tensor", lambda E, b=bank, tt=t: E.matmul(
                ps_a[b][:, 0:NE], qrT[:, tt * 128:(tt + 1) * 128], keysT[:],
                start=True, stop=True), inc="pe")
            p.wait("vector", "pe", pe_s)
            dv1 = p.op("vector", lambda E, b=bank: E.tensor_copy(s_sb[:], ps_a[b][:, 0:NE]),
                       inc="dv")
            p.release(("ps_a", bank), "dv", dv1)
            p.wait("vector", "dv", dv1)
            dv2 = p.op("vector", lambda E: E.max(m8a[:], s_sb[:]), inc="dv")
            p.wait("vector", "dv", dv2)
            dv3 = p.op("vector", lambda E: E.match_replace(s2_sb[:], m8a[:], s_sb[:], -1.0e30),
                       inc="dv")
            p.wait("vector", "dv", dv3)
            dv4 = p.op("vector", lambda E: E.max(m8b[:], s2_sb[:]), inc="dv")
            p.wait("vector", "dv", dv4)
            p.op("vector", lambda E: E.tensor_scalar_mul(c_negm1, m8a[:, 0:1], -1.0))
            dv5 = p.op("vector", lambda E: E.tensor_scalar(
                maskb[:], s_sb[:], m8b[:, 7:8], None, ALU.is_ge), inc="dv")
            p.wait("scalar", "dv", dv5)
            ac_e = p.op("scalar", lambda E: E.activation(exps[:], s_sb[:], AF.Exp,
                                                         bias=c_negm1), inc="ac")
            p.wait("vector", "ac", ac_e)
            dv6 = p.op("vector", lambda E: E.scalar_tensor_tensor(
                gnum[:], exps[:], 1.0, maskb[:], ALU.mult, ALU.mult, accum_out=c_Z64),
                inc="dv")
            p.wait("vector", "dv", dv6)
            dv7 = p.op("vector", lambda E: E.reciprocal(c_rz64, c_Z64), inc="dv")
            p.wait("vector", "dv", dv7)
            p.op("vector", lambda E, tt=t: E.tensor_scalar(
                gate[:, tt * NE:(tt + 1) * NE], gnum[:], c_rz64, 2.0, ALU.mult, ALU.mult),
                inc="dv")
        dv_gate = p.cur("dv")
        zbanks = [("ps_a", 0), ("ps_a", 1), ("ps_c", 0), ("ps_c", 1)]
        zb_ap = [ps_a[0], ps_a[1], ps_c[0], ps_c[1]]
        for tg in range(2):
            for ti in range(4):
                p.need("tensor", zbanks[ti])
            for quarter in range(4):
                par = quarter % 2
                p.need("gpsimd", ("astream", par))
                gq = p.op("gpsimd", lambda E, qq=quarter, pp=par: E.dma_start(
                    out=aq_view[pp],
                    in_=dawt.rearrange("(q p) n -> p q n", p=128)[:, qq * 8:(qq + 1) * 8, :]),
                    inc="dga%d" % par, incby=16)
                p.wait("tensor", "dga%d" % par, gq)
                pe_zq = None
                for k8 in range(8):
                    k = quarter * 8 + k8
                    for ti in range(4):
                        t = tg * 4 + ti
                        pe_zq = p.op("tensor", lambda E, b=zb_ap[ti], kk=k, k8_=k8, pp=par,
                                     tt=t:
                             E.matmul(b[:], xnT[:, kk, tt * 128:(tt + 1) * 128],
                                      aq_view[pp][:, k8_, :],
                                      start=(kk == 0), stop=(kk == NK - 1)),
                             inc="pe" if (k8 == 7 and ti == 3) else None)
                p.release(("astream", par), "pe", pe_zq)
            pe_z = p.cur("pe")
            p.wait("vector", "pe", pe_z)
            p.wait("vector", "dv", dv_gate)
            for ti in range(4):
                t = tg * 4 + ti
                zpar = ti % 2
                p.need("vector", ("zgtok", zpar))
                gview = gate[:, t * NE:(t + 1) * NE].unsqueeze(2).broadcast_to([128, NE, RK])
                dv_zg = p.op("vector", lambda E, b=zb_ap[ti], zp=zpar, gv=gview:
                             E.tensor_tensor(
                                 zgtok[:, zp, :].rearrange("p (e r) -> p e r", r=RK),
                                 b[:].rearrange("p (e r) -> p e r", r=RK), gv, ALU.mult),
                             inc="dv")
                p.release(zbanks[ti], "dv", dv_zg)
                p.wait("tensor", "dv", dv_zg)
                tb = ti % 2
                p.need("tensor", ("ps_t", tb))
                pe_zt = None
                for j in range(4):
                    pe_zt = p.op("tensor", lambda E, b=tb, jj=j, zp=zpar: E.transpose(
                        ps_t[b][:, jj * 128:(jj + 1) * 128],
                        zgtok[:, zp, jj * 128:(jj + 1) * 128], ident[:]), inc="pe")
                p.release(("zgtok", zpar), "pe", pe_zt)
                p.wait("vector", "pe", pe_zt)
                p.need("vector", ("zgT_w",))
                dv_zt = p.op("vector", lambda E, b=tb, tt=t: E.tensor_copy(
                    zgT[:, :, tt * 128:(tt + 1) * 128],
                    ps_t[b][:, 0:512].rearrange("p (a b) -> p a b", b=128)), inc="dv")
                p.release(("ps_t", tb), "dv", dv_zt)

        for blk in range(4):
            p.release(("spb", blk), "pe", p.cur("pe"))
        for blk in range(8):
            p.release(("unb", blk), "pe", p.cur("pe"))

        qevac = {}

        def mk_qproj_dma(sp, half):
            def emit():
                p.need("gpsimd", ("wstream_h", half))
                v = p.op("gpsimd", lambda E, hh=half, ss=sp: E.dma_start(
                    out=wq_half_view[hh],
                    in_=dwq.rearrange("(k p) n -> p k n", p=128)
                        [:, hh * 16:(hh + 1) * 16, ss * 256:(ss + 1) * 256]),
                    inc="dgw%d" % half, incby=16)
                qdma[(sp, half)] = v
            return emit

        qproj_last_pe = [0]

        def mk_qproj_group(sp, m, tt):
            def emit():
                h = 2 * sp + m
                bank = tt
                p.need("tensor", ("ps_a", bank))
                pe_qp = None
                for k in range(NK):
                    half = k // 16
                    p.wait("tensor", "dgw%d" % half, qdma[(sp, half)])
                    pe_qp = p.op("tensor", lambda E, b=bank, kk=k, hh=half, mm=m, t2=tt:
                                 E.matmul(
                        ps_a[b][:], wq_half_view[hh][:, kk % 16, mm * 128:(mm + 1) * 128],
                        xnT[:, kk, t2 * 512:(t2 + 1) * 512],
                        start=(kk == 0), stop=(kk == NK - 1)),
                        inc="pe" if k == NK - 1 else None)
                qproj_last_pe[0] = pe_qp
                if m == 1 and tt == 1:
                    p.release(("wstream_h", 0), "pe", pe_qp)
                    p.release(("wstream_h", 1), "pe", pe_qp)
                p.wait("vector", "pe", pe_qp)
                p.need("vector", ("QC_w",))
                dv_qe = p.op("vector", lambda E, b=bank, hh=h, t2=tt: E.tensor_scalar_mul(
                    QC[:, hh, t2 * 512:(t2 + 1) * 512], ps_a[b][:], SCALE), inc="dv")
                p.release(("ps_a", bank), "dv", dv_qe)
                qevac[(h, tt)] = dv_qe
            return emit

        qdma = {}
        qpend = []
        qsup_done = -1
        for sp in range(16):
            qpend.append(("d", sp, mk_qproj_dma(sp, 0)))
            qpend.append(("d", sp, mk_qproj_dma(sp, 1)))
            for m in range(2):
                for tt in range(2):
                    qpend.append(("g", sp, mk_qproj_group(sp, m, tt)))

        def qpop(n=1):
            nonlocal_done = 0
            while qpend and nonlocal_done < n:
                kind, sp_, fn = qpend.pop(0)
                fn()
                if kind == "g":
                    nonlocal_done += 1

        def qdrain_through(sp):
            while qpend and qpend[0][1] <= sp:
                kind, sp_, fn = qpend.pop(0)
                fn()

        for sem in ("dsr0", "dsr1", "do0", "do1"):
            p.wait("sync", sem, p.cur(sem))
        p.need("sync", ("abbuf",))
        ab_ds = p.op("sync", lambda E: E.dma_start(out=v_ab, in_=dab[0]),
                     inc="dsab", incby=16)
        scnt = 0
        for h in range(H):
            qdrain_through(h // 2)
            p.wait("vector", "dsab", ab_ds)
            W = WIN[h]
            narrow = (W + 255) <= 1024
            if h > 0 and (narrow != ((WIN[h - 1] + 255) <= 1024)):
                while tv_pend:
                    emit_tv(tv_pend.pop(0))
            wh = 1024 if narrow else 2048
            nun = 8192 // wh
            nsp = 4096 // wh
            lag = 2 if narrow else 1
            qt_info = []
            for qt in range(NT):
                q0 = tok0 + qt * 128
                kb_lo = max(0, ((q0 - W) // 128) * 128) if q0 - W > 0 else 0
                qt_info.append((q0, kb_lo, q0 + 128 - kb_lo))
            last_tt_dv = None
            if h == 0:
                un_ready = {}
                un_view = {}
                tv_pend = []
                sB_loaded = False
            if h >= 15:
                if not sB_loaded:
                    sB_loaded = True
                    while tv_pend:
                        emit_tv(tv_pend.pop(0))
                    p.need("gpsimd", ("abbuf",))
                    for sl_ in (0, 1, 2):
                        p.need("gpsimd", ("xrd", sl_))
                    for sl_ in (0, 1):
                        p.need("gpsimd", ("ost", sl_))
                    for sem_ in ("dsab", "dsr0", "dsr1", "dsr2", "do0", "do1"):
                        p.wait("gpsimd", sem_, p.cur(sem_))
                    g_trx = p.op("gpsimd", lambda E: E.dma_start(out=v_trx, in_=dtrx),
                                 inc="dgp", incby=16)
                    g_bc = p.op("gpsimd", lambda E: E.dma_start(
                        out=v_bc, in_=dbc), inc="dgp", incby=16)
                    p.wait("vector", "dgp", p.cur("dgp"))
                    p.wait("scalar", "dgp", p.cur("dgp"))
                    p.wait("scalar", "dsab", p.cur("dsab"))
                    p.wait("tensor", "dv", dv_ones)
                    p.release(("xstage", 0), "ac", p.cur("ac"))
                    p.release(("xstage_dv", 0), "ac", p.cur("ac"))
                    if c == 0:
                        while qpend:
                            kind_, sp_, fn_ = qpend.pop(0)
                            fn_()
                        p.release(("xnT_w",), "pe", qproj_last_pe[0])
                        p.release(("xnstage",), "pe", p.cur("pe"))
                zr_f = v_zrow[0]
                zr_f2 = v_zrow[1]
                zr_b = scratch[0:1, 5568:5824].bitcast(BF16)
                zb_sb = astream[:, 2048:2560]
                for g2 in range(2):
                    qt0 = 4 * g2
                    q_g = tok0 + qt0 * 128
                    kb0 = qt_info[qt0][1]
                    jlo = kb0 // 128
                    jhi = (q_g + 512) // 128
                    avbank = g2 % 2
                    p.wait("tensor", "dv", qevac[(h, g2)])
                    first = True
                    pend = []
                    use4 = not qpend
                    sbanks = ([("ps_s", 0, ps_s[0]), ("ps_s", 1, ps_s[1]),
                               ("ps_a", 0, ps_a[0]), ("ps_a", 1, ps_a[1])]
                              if use4 else
                              [("ps_s", 0, ps_s[0]), ("ps_s", 1, ps_s[1])])
                    lag = 1
                    slot_blks = (0, 1, 3)

                    def emit_avz(item, jhi=jhi, avbank=avbank):
                        nonlocal first
                        pj, pblk, pac = item
                        p.wait("tensor", "ac", pac)
                        if first:
                            p.need("tensor", ("ps_c", avbank))
                            p.need("tensor", ("ps_t", 0))
                        pe_av = p.op("tensor", lambda E, b=avbank, jj_=pj, ss=pblk,
                                     st=first, fin=(pj == jhi - 1): E.matmul(
                            ps_c[b][:, 0:512], vT[:, jj_ * 128:(jj_ + 1) * 128],
                            astream[:, ss * 1024:ss * 1024 + 512],
                            start=st, stop=fin), inc="pe")
                        pe_zl = p.op("tensor", lambda E, ss=pblk, st=first,
                                     fin=(pj == jhi - 1): E.matmul(
                            ps_tf[0][0:1, 0:512], onescol[:],
                            astream[:, ss * 1024:ss * 1024 + 512],
                            start=st, stop=fin), inc="pe")
                        p.release(("unb", pblk), "pe", pe_zl)
                        first = False
                        return pe_zl

                    for j in range(jlo, jhi):
                        bkey, bidx, bap = sbanks[scnt % len(sbanks)]
                        scnt += 1
                        p.need("tensor", (bkey, bidx))
                        pe_sc = p.op("tensor", lambda E, b=bap, hh=h, jj_=j,
                                     qq=qt0: E.matmul(
                            b[:, 0:512], kT[:, jj_ * 128:(jj_ + 1) * 128],
                            QC[:, hh, qq * 128:qq * 128 + 512],
                            start=True, stop=True), inc="pe")
                        gate_sem, gate_val = "pe", pe_sc
                        jj = j - q_g // 128
                        if jj >= 0:
                            off = 384 - 128 * jj
                            p.wait("vector", "pe", pe_sc)
                            gate_val = p.op("vector", lambda E, b=bap, oo=off:
                                            E.tensor_tensor(
                                b[:, 0:512], b[:, 0:512],
                                v_trx[:, oo:oo + 512], ALU.add), inc="dv")
                            gate_sem = "dv"
                        blk = slot_blks[j % 3]
                        p.wait("scalar", gate_sem, gate_val)
                        p.need("scalar", ("unb", blk))
                        idx = (j - 8 * c - qt0 - 3) + 17
                        ac_exp = p.op("scalar", lambda E, b=bap, ss=blk, hh=h,
                                      ii=idx: E.activation(
                            astream[:, ss * 1024:ss * 1024 + 512], b[:, 0:512],
                            AF.Exp, bias=v_bc[:, hh, ii:ii + 1]), inc="ac")
                        p.release((bkey, bidx), "ac", ac_exp)
                        pend.append((j, blk, ac_exp))
                        while len(pend) > lag:
                            pe_z = emit_avz(pend.pop(0))
                    while pend:
                        pe_z = emit_avz(pend.pop(0))
                    p.wait("vector", "pe", pe_z)
                    dv_zr = p.op("vector", lambda E: E.tensor_copy(
                        zr_f[:, 0:512], ps_tf[0][0:1, 0:512]), inc="dv")
                    p.release(("ps_t", 0), "dv", dv_zr)
                    p.wait("vector", "dv", dv_zr)
                    dv_rc = p.op("vector", lambda E: E.reciprocal(zr_f2[:, 0:512],
                                                                  zr_f[:, 0:512]),
                                 inc="dv")
                    p.wait("vector", "dv", dv_rc)
                    dv_rc = p.op("vector", lambda E: E.tensor_copy(zr_b, zr_f2[:, 0:512]),
                                 inc="dv")
                    p.wait("tensor", "dv", dv_rc)
                    p.need("tensor", ("ps_t", 1))
                    pe_bc = p.op("tensor", lambda E: E.matmul(
                        ps_tf[1][:, 0:512], onesr[:], zr_b, start=True, stop=True),
                        inc="pe")
                    p.wait("vector", "pe", pe_bc)
                    p.need("vector", ("unb", 2))
                    dv_zb = p.op("vector", lambda E: E.tensor_copy(zb_sb, ps_tf[1][:]),
                                 inc="dv")
                    p.release(("ps_t", 1), "dv", dv_zb)
                    p.wait("vector", "dv", dv_zb)
                    p.need("vector", ("QC_w",))
                    dv_ctx = p.op("vector", lambda E, b=avbank, hh=h, gg=g2:
                                  E.tensor_tensor(
                        QC[:, hh, gg * 512:(gg + 1) * 512], ps_c[b][:, 0:512],
                        zb_sb, ALU.mult), inc="dv")
                    p.release(("ps_c", avbank), "dv", dv_ctx)
                    p.release(("unb", 2), "dv", dv_ctx)
                    if g2 == 0:
                        qpop(1)
                if c == 0 and 16 <= h <= 23:
                    ln1_dv[0] = emit_ln_tile(CH, h - 16)
                continue

            def emit_tv(args):
                hh_, g_, qt_info_ = args
                q00, kb0, _ = qt_info_[2 * g_]
                q01, kb1, _ = qt_info_[2 * g_ + 1]
                jlo = min(kb0, kb1) // 128
                jhi = (q01 + 128) // 128
                avbank = g_ % 2
                first_av = True
                pe_av_last = None
                pend_av = None
                pairs = [list(range(j, min(j + 2, jhi))) for j in range(jlo, jhi, 2)]
                last_tp_qt = {}
                for pj in pairs + [None]:
                    if pj is not None:
                        tb = (pj[0] // 2) % 2
                        p.need("tensor", ("ps_t", tb))
                        pe_tp = None
                        ops = []
                        for jr, j in enumerate(pj):
                            for qi in (0, 1):
                                qt = 2 * g_ + qi
                                if not (qt_info_[qt][1] // 128 <= j
                                        <= qt_info_[qt][0] // 128):
                                    continue
                                off = j * 128 - qt_info_[qt][1]
                                ops.append((jr, qi, qt, off))
                        for on, (jr, qi, qt, off) in enumerate(ops):
                            p.wait("tensor", "ac", un_ready[(hh_, qt)])
                            pe_tp = p.op("tensor", lambda E, b=tb, col=jr * 256 + qi * 128,
                                         uv=un_view[(hh_, qt)], oo=off: E.transpose(
                                ps_t[b][:, col:col + 128],
                                uv[:, oo:oo + 128], ident[:]), inc="pe")
                            last_tp_qt[qt] = pe_tp
                        ev_eng = "vector" if tb == 0 else "scalar"
                        ev_sem = "dv" if tb == 0 else "ac"
                        p.wait(ev_eng, "pe", pe_tp)
                        p.need(ev_eng, ("ut", tb))
                        if ev_eng == "vector":
                            cpf = lambda E, o, i: E.tensor_copy(o, i)
                            msf = lambda E, o: E.memset(o, 0.0)
                        else:
                            cpf = lambda E, o, i: E.copy(o, i)
                            msf = lambda E, o: E.memzero(o)
                        wid = len(pj) * 256
                        ev = p.op(ev_eng, lambda E, b=tb, u=v_ut[tb], w=wid, f=cpf:
                                  f(E, u[:, 0:w], ps_t[b][:, 0:w]), inc=ev_sem)
                        have = {(jr, qi) for (jr, qi, qt, off) in ops}
                        for jr, j in enumerate(pj):
                            for qi in (0, 1):
                                if (jr, qi) not in have:
                                    p.wait(ev_eng, ev_sem, ev)
                                    ev = p.op(ev_eng, lambda E, u=v_ut[tb],
                                              col=jr * 256 + qi * 128, f=msf: f(
                                        E, u[:, col:col + 128]), inc=ev_sem)
                        p.release(("ps_t", tb), ev_sem, ev)
                    if pend_av is not None:
                        pj_, tb_, es_, evv_ = pend_av
                        p.wait("tensor", es_, evv_)
                        if first_av:
                            p.need("tensor", ("ps_c", avbank))
                        for jr, j in enumerate(pj_):
                            pe_av_last = p.op("tensor", lambda E, b=avbank, jj=j,
                                              u=v_ut[tb_], c0=jr * 256, st=first_av,
                                              fin=(jj == jhi - 1) if False else (j == jhi - 1):
                                              E.matmul(
                                ps_c[b][:, 0:256], vT[:, jj * 128:(jj + 1) * 128],
                                u[:, c0:c0 + 256], start=st, stop=fin), inc="pe")
                            first_av = False
                        p.release(("ut", tb_), "pe", pe_av_last)
                    if pj is not None:
                        pend_av = (pj, tb, ev_sem, ev)
                p.wait("scalar", "pe", pe_av_last)
                ac_ctx = p.op("scalar", lambda E, b=avbank, hh=hh_, gg=g_: E.copy(
                    QC[:, hh, gg * 256:(gg + 1) * 256], ps_c[b][:, 0:256]), inc="ac")
                p.release(("ps_c", avbank), "ac", ac_ctx)
                for qtr in (2 * g_, 2 * g_ + 1):
                    uv = un_view[(hh_, qtr)]
                    b0 = uv.offset // 1024
                    nb = uv.shape[-1] // 1024
                    for blk in range(b0, b0 + nb):
                        p.release(("unb", blk), "pe", pe_av_last)

            for g in range(4):
                if g < 3:
                    qpop(1)
                for qi in range(2):
                    qt = 2 * g + qi
                    q0, kb_lo, L = qt_info[qt]
                    nch = (L + 511) // 512
                    spar = qt % nsp
                    us = qt % nun
                    vsp = scratch[:, spar * wh:spar * wh + wh]
                    vun = astream[:, us * wh:us * wh + wh]
                    un_view[(h, qt)] = vun
                    for blk in range(spar * wh // 1024, (spar + 1) * wh // 1024):
                        p.need("vector", ("spb", blk))
                    p.wait("tensor", "dv", qevac[(h, qt // 4)])
                    for ci in range(nch):
                        colw = min(512, L - ci * 512)
                        bank = scnt % 2
                        scnt += 1
                        p.need("tensor", ("ps_s", bank))
                        pe_sc = p.op("tensor", lambda E, b=bank, hh=h, qq=qt, k0=kb_lo,
                                     cc=ci, w=colw: E.matmul(
                            ps_s[b][:, 0:w], QC[:, hh, qq * 128:(qq + 1) * 128],
                            kT[:, k0 + cc * 512: k0 + cc * 512 + w],
                            start=True, stop=True), inc="pe")
                        p.wait("vector", "pe", pe_sc)
                        aboff = kb_lo - q0 + S + ci * 512
                        dv_tt = p.op("vector", lambda E, b=bank, cc=ci, w=colw, vv=vsp,
                                     ao=aboff: E.tensor_tensor(
                                         vv[:, cc * 512:cc * 512 + w],
                                         ps_s[b][:, 0:w], v_ab[:, ao:ao + w], ALU.add),
                                     inc="dv")
                        p.release(("ps_s", bank), "dv", dv_tt)
                        last_tt_dv = dv_tt
                    p.wait("scalar", "dv", last_tt_dv)
                    for blk in range(us * wh // 1024, (us + 1) * wh // 1024):
                        p.need("scalar", ("unb", blk))
                    ac_exp = p.op("scalar", lambda E, uv=vun, vv=vsp, ss=us, LL=L:
                                  E.activation(
                        uv[:, 0:LL], vv[:, 0:LL], AF.Exp,
                        bias=c_negB, accum_out=c_Z[ss]), inc="ac")
                    for blk in range(spar * wh // 1024, (spar + 1) * wh // 1024):
                        p.release(("spb", blk), "ac", ac_exp)
                    p.wait("vector", "ac", ac_exp)
                    dv_rz = p.op("vector", lambda E, ss=us: E.reciprocal(
                        c_rz[ss], c_Z[ss]), inc="dv")
                    p.wait("scalar", "dv", dv_rz)
                    un_ready[(h, qt)] = p.op("scalar", lambda E, uv=vun, ss=us, LL=L:
                                             E.activation(
                        uv[:, 0:LL], uv[:, 0:LL], AF.Identity, scale=c_rz[ss]), inc="ac")
                tv_pend.append((h, g, qt_info))
                if len(tv_pend) > lag:
                    emit_tv(tv_pend.pop(0))
            if h == H - 1:
                while tv_pend:
                    emit_tv(tv_pend.pop(0))
            nxt = h + 1 if h + 1 <= 14 else None
            p.release(("abbuf",), "dv", last_tt_dv)
            if nxt is not None:
                p.need("sync", ("abbuf",))
                ab_ds = p.op("sync", lambda E, hh=nxt: E.dma_start(out=v_ab, in_=dab[hh]),
                             inc="dsab", incby=16)
        while qpend:
            kind, sp_, fn = qpend.pop(0)
            fn()
        p.release(("wo", 0), "pe", qproj_last_pe[0])
        p.release(("wo", 1), "pe", qproj_last_pe[0])

        ac_attn_end = p.cur("ac")
        dv_attn_end = p.cur("dv")
        for sl in (0, 1, 2):
            p.release(("xrd", sl), "dv", dv_attn_end)
        for sl in (0, 1):
            p.release(("ost", sl), "dv", dv_attn_end)
        p.release(("xnstage",), "pe", p.cur("pe"))
        p.release(("astream", 0), "pe", p.cur("pe"))
        p.release(("astream", 1), "pe", p.cur("pe"))
        p.release(("abbuf",), "dv", dv_attn_end)

        p.wait("tensor", "ac", ac_attn_end)
        p.wait("sync", "dsab", p.cur("dsab"))
        p.wait("sync", "ac", p.cur("ac"))
        p.wait("sync", "dgp", p.cur("dgp"))
        p.wait("vector", "dsab", p.cur("dsab"))
        p.wait("scalar", "dsab", p.cur("dsab"))
        p.wait("gpsimd", "pe", p.cur("pe"))
        p.wait("gpsimd", "dv", p.cur("dv"))
        p.wait("gpsimd", "ac", p.cur("ac"))
        pe_fin_last = None
        xrd_ds = {}
        fbanks = [("ps_a", 0, ps_a[0]), ("ps_a", 1, ps_a[1]),
                  ("ps_c", 0, ps_c[0]), ("ps_c", 1, ps_c[1]),
                  ("ps_s", 0, ps_s[0]), ("ps_s", 1, ps_s[1]),
                  ("ps_t", 0, ps_tf[0]), ("ps_t", 1, ps_tf[1])]
        for fsp in range(8):
            par = fsp % 2
            p.need("gpsimd", ("wstream_h", par))
            g_b = p.op("gpsimd", lambda E, pp=par, ff=fsp: E.dma_start(
                out=b_view[pp],
                in_=dbwt.rearrange("(k p) n -> p k n", p=128)[:, :, ff * 512:(ff + 1) * 512]),
                inc="dgw%d" % par, incby=16)
            for t in range(NT):
                p.need("tensor", fbanks[t][:2])
            def issue_xrd(t, ff=fsp):
                sl = t % 3
                p.need("sync", ("xrd", sl))
                xrd_ds[t] = p.op("sync", lambda E, r=tok0 + t * 128, ss=sl:
                                 E.dma_start(
                    out=v_xrd[ss], in_=dx[r:r + 128, ff * 512:(ff + 1) * 512]),
                    inc="dsr%d" % sl, incby=16)

            for quarter in range(4):
                sl2 = quarter % 2
                p.need("gpsimd", ("astream", sl2))
                g_wo = p.op("gpsimd", lambda E, qq=quarter, ss=sl2, ff=fsp:
                            E.dma_start(
                    out=wo_slab[ss],
                    in_=dwo.rearrange("(q p) n -> p q n", p=128)
                        [:, qq * 8:(qq + 1) * 8, ff * 512:(ff + 1) * 512]),
                    inc="dgo%d" % sl2, incby=16)
                p.wait("tensor", "dgo%d" % sl2, g_wo)
                if quarter == 1:
                    issue_xrd(0)
                    issue_xrd(1)
                if quarter == 2:
                    issue_xrd(2)
                    p.wait("tensor", "dgw%d" % par, g_b)
                pe_q = None
                if quarter < 3:
                    for k8 in range(8):
                        k = quarter * 8 + k8
                        for t in range(NT):
                            pe_q = p.op("tensor", lambda E, b=fbanks[t][2], kk=k,
                                        k8_=k8, ss=sl2, tt=t: E.matmul(
                                b[:], QC[:, kk, tt * 128:(tt + 1) * 128],
                                wo_slab[ss][:, k8_, :],
                                start=(kk == 0), stop=False),
                                inc="pe" if (k8 == 7 and t == NT - 1) else None)
                    p.release(("astream", sl2), "pe", pe_q)
                    continue
                for t in range(NT):
                    sl = t % 3
                    for k8 in range(8):
                        p.op("tensor", lambda E, b=fbanks[t][2], kk=24 + k8,
                             k8_=k8, ss=sl2, tt=t: E.matmul(
                            b[:], QC[:, kk, tt * 128:(tt + 1) * 128],
                            wo_slab[ss][:, k8_, :], start=False, stop=False))
                    pe_f = None
                    for n in range(4):
                        pe_f = p.op("tensor", lambda E, b=fbanks[t][2], nn=n, pp=par,
                                    tt=t: E.matmul(
                            b[:], zgT[:, nn, tt * 128:(tt + 1) * 128], b_view[pp][:, nn, :],
                            start=False, stop=(nn == 3)),
                            inc="pe" if n == 3 else None)
                    pe_fin_last = pe_f
                    p.wait("vector", "pe", pe_f)
                    p.wait("vector", "dsr%d" % sl, xrd_ds[t])
                    p.need("vector", ("ost", t % 2))
                    dv_o = p.op("vector", lambda E, b=fbanks[t][2], ss=sl, so=t % 2:
                                E.tensor_add(
                        v_ost[so], b[:], v_xrd[ss]), inc="dv")
                    p.release(fbanks[t][:2], "dv", dv_o)
                    p.release(("xrd", sl), "dv", dv_o)
                    p.wait("scalar", "dv", dv_o)
                    do_v = p.op("scalar", lambda E, r=tok0 + t * 128, so=t % 2, ff=fsp:
                                E.dma_start(
                        out=dout[r:r + 128, ff * 512:(ff + 1) * 512], in_=v_ost[so]),
                        inc="do%d" % (t % 2), incby=16)
                    p.release(("ost", t % 2), "do%d" % (t % 2), do_v)
                    if t + 3 < NT:
                        issue_xrd(t + 3)
                p.release(("astream", sl2), "pe", pe_fin_last)
            p.release(("wstream_h", par), "pe", pe_fin_last)
        p.release(("xnT_w",), "pe", pe_fin_last)
        p.release(("zgT_w",), "pe", pe_fin_last)
        p.release(("QC_w",), "pe", pe_fin_last)


    p.wait("scalar", "do0", p.cur("do0"))
    p.wait("scalar", "do1", p.cur("do1"))

    with ctx:
        with nc.Block() as block:
            block.sync(lambda E: [f(E) for f in p.q["sync"]])
            block.gpsimd(lambda E: [f(E) for f in p.q["gpsimd"]])
            block.tensor(lambda E: [f(E) for f in p.q["tensor"]])
            block.vector(lambda E: [f(E) for f in p.q["vector"]])
            block.scalar(lambda E: [f(E) for f in p.q["scalar"]])
    return nc


_CACHED = {}


def _host_constants():
    if "ab" not in _CACHED:
        krel = np.arange(-S, 128, dtype=np.float32)
        i = np.arange(128, dtype=np.float32)
        diff = krel[None, :] - i[:, None]
        ab = np.empty((H, 128, S + 128), dtype=np.float32)
        for h in range(H):
            ab[h] = np.where(diff <= 0, SLOPES[h] * diff, NEG)
        _CACHED["ab"] = ab
        _CACHED["ident"] = np.eye(128, dtype=np.float32)
        x_ = np.arange(128, dtype=np.float32)
        bc = np.empty((128, H, 18), dtype=np.float32)
        for h in range(H):
            for idx in range(18):
                m = idx - 17
                bc[:, h, idx] = SLOPES[h] * (x_ + 128.0 * m - 127.0) - 30.0
        _CACHED["bc"] = bc
        z_ = np.arange(896, dtype=np.float32)
        _CACHED["trx"] = np.where(x_[:, None] > z_[None, :] - 384.0, NEG, 0.0
                                  ).astype(np.float32)
    return _CACHED


def kernel(L0_attn_hidden_states, L0_attn_Wq, L0_attn_Wkv, L0_attn_Wo,
           L0_ffn_A_w, L0_ffn_B_w, L0_ffn_Wq_w, L0_ffn_keys):
    x = np.ascontiguousarray(np.asarray(L0_attn_hidden_states, dtype=np.float32))
    cst = _host_constants()
    shared = {
        "wq": np.ascontiguousarray(np.asarray(L0_attn_Wq, dtype=np.float32)),
        "wkv": np.ascontiguousarray(np.asarray(L0_attn_Wkv, dtype=np.float32)),
        "wo": np.ascontiguousarray(np.asarray(L0_attn_Wo, dtype=np.float32)),
        "awt": np.ascontiguousarray(np.asarray(L0_ffn_A_w, dtype=np.float32).T),
        "bwt": np.ascontiguousarray(np.asarray(L0_ffn_B_w, dtype=np.float32).T),
        "wqwt": np.ascontiguousarray(np.asarray(L0_ffn_Wq_w, dtype=np.float32).T),
        "keyst": np.ascontiguousarray(np.asarray(L0_ffn_keys, dtype=np.float32).T),
        "ab": cst["ab"], "ident": cst["ident"], "bc": cst["bc"], "trx": cst["trx"],
    }
    nc = build_program()
    in_maps = [dict(shared, x=np.ascontiguousarray(x[i])) for i in range(8)]
    res = run_bass_kernel_spmd(nc, in_maps, core_ids=list(range(8)))
    return np.stack([r["out"] for r in res.results], axis=0)


# version 22 (final)
# speedup vs baseline: 1.2122x; 1.2122x over previous
import math
import numpy as np
import concourse.bass as bass
import concourse.mybir as mybir
from concourse.bass_utils import run_bass_kernel_spmd
from contextlib import ExitStack

F32 = mybir.dt.float32
BF16 = mybir.dt.bfloat16
AF = mybir.ActivationFunctionType
ALU = mybir.AluOpType
AX = mybir.AxisListType

S = 2048
D = 4096
H = 32
HD = 128
CH = 1024
NT = CH // 128
NE = 64
RK = 8
RD = 16
NK = D // 128
SCALE = HD ** -0.5
SLOPES = [2.0 ** (-(i + 1) / 4.0) for i in range(H)]
WIN = [int(math.ceil(140.0 / s)) for s in SLOPES]
NEG = -1.0e9


class P:

    ENG = ("sync", "gpsimd", "tensor", "vector", "scalar")

    def __init__(self):
        self.q = {e: [] for e in self.ENG}
        self.ctr = {}
        self.lastwait = {e: {} for e in self.ENG}
        self.sems = {}
        self.mmrun = 0
        self.free = {}

    def inc(self, sem, by=1):
        self.ctr[sem] = self.ctr.get(sem, 0) + by
        return self.ctr[sem]

    def cur(self, sem):
        return self.ctr.get(sem, 0)

    def wait(self, eng, sem, val):
        if val is None or val <= 0:
            return
        if sem.startswith(("ds", "dg", "do")):
            val = self.ctr.get(sem, 0)
        lw = self.lastwait[eng]
        if lw.get(sem, 0) >= val:
            return
        lw[sem] = val
        self.q[eng].append(lambda E, s=sem, v=val: E.wait_ge(self.sems[s], v))

    def op(self, eng, fn, inc=None, incby=1):
        if eng == "tensor":
            self.mmrun += 1
            if self.mmrun >= 12:
                self.mmrun = 0
                self.q[eng].append(lambda E: E.nop())
        if inc is None:
            self.q[eng].append(fn)
            return None
        v = self.inc(inc, incby)
        self.q[eng].append(lambda E, s=inc, b=incby: fn(E).then_inc(self.sems[s], b))
        return v

    def need(self, eng, key):
        if key in self.free:
            sem, val = self.free[key]
            self.wait(eng, sem, val)

    def release(self, key, sem, val):
        self.free[key] = (sem, val)


def build_program():
    nc = bass.Bass("TRN2", target_bir_lowering=False)
    p = P()

    dx = nc.dram_tensor("x", [S, D], F32, kind="ExternalInput").ap()
    dwq = nc.dram_tensor("wq", [D, D], F32, kind="ExternalInput").ap()
    dwkv = nc.dram_tensor("wkv", [D, 2 * HD], F32, kind="ExternalInput").ap()
    dwo = nc.dram_tensor("wo", [D, D], F32, kind="ExternalInput").ap()
    dawt = nc.dram_tensor("awt", [D, 512], F32, kind="ExternalInput").ap()
    dbwt = nc.dram_tensor("bwt", [512, D], F32, kind="ExternalInput").ap()
    dwqwt = nc.dram_tensor("wqwt", [D, RD], F32, kind="ExternalInput").ap()
    dkeyst = nc.dram_tensor("keyst", [RD, NE], F32, kind="ExternalInput").ap()
    dab = nc.dram_tensor("ab", [H, 128, S + 128], F32, kind="ExternalInput").ap()
    dbc = nc.dram_tensor("bc", [128, H, 18], F32, kind="ExternalInput").ap()
    dtrx = nc.dram_tensor("trx", [128, 896], F32, kind="ExternalInput").ap()
    dident = nc.dram_tensor("ident", [128, 128], F32, kind="ExternalInput").ap()
    dout = nc.dram_tensor("out", [S, D], F32, kind="ExternalOutput").ap()

    ctx = ExitStack()
    sb = lambda n, sh, dt: ctx.enter_context(nc.sbuf_tensor(n, sh, dt))
    psal = lambda n, sh, dt: ctx.enter_context(nc.psum_tensor(n, sh, dt))

    kT = sb("kT", [128, S], BF16)
    vT = sb("vT", [128, S], BF16)
    ident = sb("identsb", [128, 128], BF16)
    wqwT = sb("wqwT", [128, NK, RD], BF16)
    keysT = sb("keysT", [RD, NE], BF16)
    wstream = sb("wstream", [128, 8192], BF16)
    astream = sb("astream", [128, 8192], BF16)
    zgT = sb("zgT", [128, 4, CH], BF16)
    xnT = sb("xnT", [128, NK, CH], BF16)
    QC = sb("QC", [128, H, CH], BF16)
    scratch = sb("scratch", [128, 6784], F32)
    zgtok = scratch[:, 0:512].bitcast(BF16).rearrange("p (a b) -> p a b", b=512)
    gate = scratch[:, 512:1024]
    qrT = scratch[0:16, 1024:1536].bitcast(BF16)
    s_sb = sb("s_sb", [128, NE], F32)
    s2_sb = sb("s2_sb", [128, NE], F32)
    exps = sb("exps", [128, NE], F32)
    maskb = sb("maskb", [128, NE], F32)
    gnum = sb("gnum", [128, NE], F32)
    m8a = sb("m8a", [128, 8], F32)
    m8b = sb("m8b", [128, 8], F32)
    cols = sb("cols", [128, 48], F32)
    onesr = sb("onesr", [1, 128], BF16)
    onescol = sb("onescol", [128, 1], BF16)
    stats = sb("stats", [128, 8], F32)

    v_xstage2 = [scratch[:, 0:4096]]
    v_xnstage = astream[:, 4096:8192]
    v_sp = [scratch[:, 0:2048], scratch[:, 2048:4096]]
    v_ab = scratch[:, 4096:6272]
    v_un = [astream[:, s * 2048:(s + 1) * 2048] for s in range(4)]
    v_ut = [scratch[:, 6272:6528].bitcast(BF16), scratch[:, 6528:6784].bitcast(BF16)]
    v_trx = scratch[:, 4096:4992]
    v_bc = scratch[:, 4992:5568].rearrange("p (h m) -> p h m", m=18)
    v_zrow = [scratch[0:1, 5568:6080], scratch[0:1, 6080:6592]]
    v_xrd = [scratch[:, 4096:4608], scratch[:, 4608:5120]]
    v_ost = [scratch[:, 5120:5632], scratch[:, 5632:6144]]

    c_nsum = cols[:, 0:1]
    c_sumsq = cols[:, 1:2]
    c_msq = cols[:, 2:3]
    c_var = cols[:, 3:4]
    c_t1 = cols[:, 4:5]
    c_rstd = cols[:, 5:6]
    c_lbias = cols[:, 6:7]
    c_negm1 = cols[:, 7:8]
    c_Z64 = cols[:, 8:9]
    c_rz64 = cols[:, 9:10]
    c_eps = cols[:, 34:35]
    c_negB = cols[:, 35:36]
    c_lnz = cols[:, 36:37]
    c_nm = [cols[:, 10 + i:11 + i] for i in range(8)]
    c_Z = [cols[:, 18 + i:19 + i] for i in range(8)]
    c_rz = [cols[:, 26 + i:27 + i] for i in range(8)]

    ps_s = [psal("ps_s0", [128, 512], F32), psal("ps_s1", [128, 512], F32)]
    ps_tf = [psal("ps_t0", [128, 512], F32), psal("ps_t1", [128, 512], F32)]
    ps_t = [t[:].bitcast(BF16) for t in ps_tf]
    ps_c = [psal("ps_c0", [128, 512], F32), psal("ps_c1", [128, 512], F32)]
    ps_a = [psal("ps_a0", [128, 512], F32), psal("ps_a1", [128, 512], F32)]

    for name in ("pe", "dv", "ac", "dgp", "dgw0", "dgw1", "dga0", "dga1",
                 "dgo0", "dgo1", "dsx", "dsab", "dsa0", "dsa1", "dsr0", "dsr1", "do0", "do1"):
        p.sems[name] = ctx.enter_context(nc.semaphore(name))

    wq_half_view = [wstream[:, 0:4096].rearrange("p (k n) -> p k n", n=256),
                    wstream[:, 4096:8192].rearrange("p (k n) -> p k n", n=256)]
    wkv_view = wstream[:, 0:4096].rearrange("p (k n) -> p k n", n=128)
    wkv_view_v = wstream[:, 4096:8192].rearrange("p (k n) -> p k n", n=128)
    b_view = [wstream[:, 0:2048].rearrange("p (k n) -> p k n", n=512),
              wstream[:, 4096:6144].rearrange("p (k n) -> p k n", n=512)]
    aq_view = [astream[:, 0:4096].rearrange("p (k n) -> p k n", n=512),
               astream[:, 4096:8192].rearrange("p (k n) -> p k n", n=512)]
    wo_view = [xnT[:, 0:16, :].rearrange("p k n -> p (k n)").rearrange("p (k n) -> p k n", n=512),
               xnT[:, 16:32, :].rearrange("p k n -> p (k n)").rearrange("p (k n) -> p k n", n=512)]

    dv_eps = p.op("vector", lambda E: E.memset(c_eps, 1.0e-5), inc="dv")
    p.wait("vector", "dv", dv_eps)
    dv_eps = p.op("vector", lambda E: E.memset(c_negB, -40.0), inc="dv")
    p.wait("vector", "dv", dv_eps)
    dv_ones = p.op("vector", lambda E: E.memset(onesr[:], 1.0), inc="dv")
    p.wait("vector", "dv", dv_ones)
    dv_ones = p.op("vector", lambda E: E.memset(onescol[:], 1.0), inc="dv")
    ld_ident = p.op("gpsimd", lambda E: E.dma_start(out=ident[:], in_=dident),
                    inc="dgp", incby=16)
    ld_wqwt = p.op("gpsimd", lambda E: E.dma_start(
        out=wqwT[:], in_=dwqwt.rearrange("(k p) r -> p k r", p=128)), inc="dgp", incby=16)
    ld_keyst = p.op("gpsimd", lambda E: E.dma_start(out=keysT[:], in_=dkeyst),
                    inc="dgp", incby=16)

    for c in range(2):
        tok0 = c * CH
        if c > 0:
            for sem in ("dsab", "dsr0", "dsr1", "do0", "do1"):
                for eng in ("vector", "scalar", "tensor"):
                    p.wait(eng, sem, p.cur(sem))

        for t in range(NT):
            row0 = tok0 + t * 128
            xsl = 0
            v_xstage = v_xstage2[xsl]
            p.need("sync", ("xstage", xsl))
            p.need("sync", ("xstage_dv", xsl))
            v = p.op("sync", lambda E, r=row0, xs=v_xstage: E.dma_start(
                out=xs, in_=dx[r:r + 128, :]), inc="dsx", incby=16)
            p.wait("vector", "dsx", v)
            dv_red = p.op("vector", lambda E, xs=v_xstage: E.reduce_sum(
                c_nsum, xs, axis=AX.X, negate=True), inc="dv")
            p.release(("xstage_dv", xsl), "dv", dv_red)
            p.wait("scalar", "dsx", v)
            p.need("scalar", ("xnstage",))
            ac_sq = p.op("scalar", lambda E, xs=v_xstage: E.activation(
                v_xnstage, xs, AF.Square, accum_out=c_sumsq), inc="ac")
            p.wait("vector", "dv", dv_red)
            dv_msq = p.op("vector", lambda E: E.scalar_tensor_tensor(
                c_msq, c_nsum, 1.0 / (float(D) * float(D)), c_nsum, ALU.mult, ALU.mult),
                inc="dv")
            p.wait("vector", "ac", ac_sq)
            p.wait("vector", "dv", dv_msq)
            dv_var = p.op("vector", lambda E: E.scalar_tensor_tensor(
                c_var, c_sumsq, 1.0 / float(D), c_msq, ALU.mult, ALU.subtract), inc="dv")
            p.wait("scalar", "dv", dv_var)
            p.wait("scalar", "dv", dv_eps)
            ac_ln = p.op("scalar", lambda E: E.activation(c_t1, c_var, AF.Ln, bias=c_eps),
                         inc="ac")
            p.wait("scalar", "ac", ac_ln)
            ac_rstd = p.op("scalar", lambda E: E.activation(c_rstd, c_t1, AF.Exp, scale=-0.5),
                           inc="ac")
            p.wait("vector", "ac", ac_rstd)
            dv_lb = p.op("vector", lambda E: E.scalar_tensor_tensor(
                c_lbias, c_nsum, 1.0 / float(D), c_rstd, ALU.mult, ALU.mult), inc="dv")
            p.wait("scalar", "dv", dv_lb)
            ac_apply = p.op("scalar", lambda E, xs=v_xstage: E.activation(
                v_xnstage, xs, AF.Identity, bias=c_lbias, scale=c_rstd), inc="ac")
            p.release(("xstage", xsl), "ac", ac_apply)

            p.wait("tensor", "dgp", ld_ident)
            p.wait("tensor", "ac", ac_apply)
            last_pe = None
            for kb in range(8):
                bank = kb % 2
                p.need("tensor", ("ps_t", bank))
                pe_tr = None
                for j in range(4):
                    k = kb * 4 + j
                    pe_tr = p.op("tensor", lambda E, b=bank, jj=j, kk=k: E.transpose(
                        ps_t[b][:, jj * 128:(jj + 1) * 128],
                        v_xnstage[:, kk * 128:(kk + 1) * 128], ident[:]), inc="pe")
                p.wait("vector", "pe", pe_tr)
                p.need("vector", ("xnT_w",))
                dv_ev = p.op("vector", lambda E, b=bank, kb_=kb, tt=t: E.tensor_copy(
                    xnT[:, kb_ * 4:kb_ * 4 + 4, tt * 128:(tt + 1) * 128],
                    ps_t[b][:, 0:512].rearrange("p (a b) -> p a b", b=128)), inc="dv")
                p.release(("ps_t", bank), "dv", dv_ev)
                last_pe = pe_tr
            p.release(("xnstage",), "pe", last_pe)

            if t == 0:
                dv_tile_prev = p.cur("dv")
                continue
            if t == 1:
                p.need("gpsimd", ("wstream_h", 0))
                g_k = p.op("gpsimd", lambda E: E.dma_start(
                    out=wkv_view,
                    in_=dwkv.rearrange("(k p) n -> p k n", p=128)[:, :, 0:HD]),
                    inc="dgw0", incby=16)
                p.need("gpsimd", ("wstream_h", 1))
                g_v = p.op("gpsimd", lambda E: E.dma_start(
                    out=wkv_view_v,
                    in_=dwkv.rearrange("(k p) n -> p k n", p=128)[:, :, HD:2 * HD]),
                    inc="dgw1", incby=16)
            vt = t - 1
            p.wait("tensor", "dgw1", g_v)
            p.wait("tensor", "dv", dv_tile_prev)
            bank = vt % 2
            p.need("tensor", ("ps_a", bank))
            pe_v = None
            for k in range(NK):
                pe_v = p.op("tensor", lambda E, b=bank, kk=k, tt=vt: E.matmul(
                    ps_a[b][:, 0:HD], xnT[:, kk, tt * 128:(tt + 1) * 128],
                    wkv_view_v[:, kk, :],
                    start=(kk == 0), stop=(kk == NK - 1)),
                    inc="pe" if k == NK - 1 else None)
            p.wait("vector", "pe", pe_v)
            dv_v = p.op("vector", lambda E, b=bank, tt=vt, t0=tok0: E.tensor_copy(
                vT[:, t0 + tt * 128: t0 + (tt + 1) * 128], ps_a[b][:, 0:HD]), inc="dv")
            p.release(("ps_a", bank), "dv", dv_v)
            if t == 3:
                p.wait("tensor", "dgw0", g_k)
                p.wait("tensor", "dv", p.cur("dv"))
                p.need("tensor", ("ps_c", 0))
                pe_k = None
                for k in range(NK):
                    pe_k = p.op("tensor", lambda E, kk=k: E.matmul(
                        ps_c[0][:], wkv_view[:, kk, :], xnT[:, kk, 0:512],
                        start=(kk == 0), stop=(kk == NK - 1)),
                        inc="pe" if k == NK - 1 else None)
                p.wait("vector", "pe", pe_k)
                dv_k = p.op("vector", lambda E, t0=tok0: E.tensor_copy(
                    kT[:, t0:t0 + 512], ps_c[0][:]), inc="dv")
                p.release(("ps_c", 0), "dv", dv_k)
            dv_tile_prev = p.cur("dv")
        dv_xnT_done = p.cur("dv")

        p.wait("tensor", "dgw0", g_k)
        p.wait("tensor", "dv", dv_xnT_done)
        for half in (1,):
            bank = half % 2
            p.need("tensor", ("ps_a", bank))
            pe_k = None
            for k in range(NK):
                pe_k = p.op("tensor", lambda E, b=bank, kk=k, hh=half: E.matmul(
                    ps_a[b][:], wkv_view[:, kk, :], xnT[:, kk, hh * 512:(hh + 1) * 512],
                    start=(kk == 0), stop=(kk == NK - 1)),
                    inc="pe" if k == NK - 1 else None)
            p.wait("vector", "pe", pe_k)
            dv_k = p.op("vector", lambda E, b=bank, hh=half, t0=tok0: E.tensor_copy(
                kT[:, t0 + hh * 512: t0 + (hh + 1) * 512], ps_a[b][:]), inc="dv")
            p.release(("ps_a", bank), "dv", dv_k)
        for t in (NT - 1,):
            bank = t % 2
            p.need("tensor", ("ps_a", bank))
            pe_v = None
            for k in range(NK):
                pe_v = p.op("tensor", lambda E, b=bank, kk=k, tt=t: E.matmul(
                    ps_a[b][:, 0:HD], xnT[:, kk, tt * 128:(tt + 1) * 128],
                    wkv_view_v[:, kk, :],
                    start=(kk == 0), stop=(kk == NK - 1)),
                    inc="pe" if k == NK - 1 else None)
            p.wait("vector", "pe", pe_v)
            dv_v = p.op("vector", lambda E, b=bank, tt=t, t0=tok0: E.tensor_copy(
                vT[:, t0 + tt * 128: t0 + (tt + 1) * 128], ps_a[b][:, 0:HD]), inc="dv")
            p.release(("ps_a", bank), "dv", dv_v)
        p.release(("wstream_h", 0), "pe", p.cur("pe"))
        p.release(("wstream_h", 1), "pe", p.cur("pe"))
        p.wait("tensor", "dgp", ld_wqwt)
        for half in range(2):
            bank = half % 2
            p.need("tensor", ("ps_a", bank))
            pe_q = None
            for k in range(NK):
                pe_q = p.op("tensor", lambda E, b=bank, kk=k, hh=half: E.matmul(
                    ps_a[b][0:RD, :], wqwT[:, kk, :], xnT[:, kk, hh * 512:(hh + 1) * 512],
                    start=(kk == 0), stop=(kk == NK - 1)),
                    inc="pe" if k == NK - 1 else None)
            p.wait("vector", "pe", pe_q)
            dv_q = p.op("vector", lambda E, b=bank, hh=half: E.tensor_copy(
                qrT[:, hh * 512:(hh + 1) * 512], ps_a[b][0:RD, :]), inc="dv")
            p.release(("ps_a", bank), "dv", dv_q)
        dv_qrt = p.cur("dv")
        p.wait("tensor", "dgp", ld_keyst)
        p.wait("tensor", "dv", dv_qrt)
        for t in range(NT):
            bank = t % 2
            p.need("tensor", ("ps_a", bank))
            pe_s = p.op("tensor", lambda E, b=bank, tt=t: E.matmul(
                ps_a[b][:, 0:NE], qrT[:, tt * 128:(tt + 1) * 128], keysT[:],
                start=True, stop=True), inc="pe")
            p.wait("vector", "pe", pe_s)
            dv1 = p.op("vector", lambda E, b=bank: E.tensor_copy(s_sb[:], ps_a[b][:, 0:NE]),
                       inc="dv")
            p.release(("ps_a", bank), "dv", dv1)
            p.wait("vector", "dv", dv1)
            dv2 = p.op("vector", lambda E: E.max(m8a[:], s_sb[:]), inc="dv")
            p.wait("vector", "dv", dv2)
            dv3 = p.op("vector", lambda E: E.match_replace(s2_sb[:], m8a[:], s_sb[:], -1.0e30),
                       inc="dv")
            p.wait("vector", "dv", dv3)
            dv4 = p.op("vector", lambda E: E.max(m8b[:], s2_sb[:]), inc="dv")
            p.wait("vector", "dv", dv4)
            p.op("vector", lambda E: E.tensor_scalar_mul(c_negm1, m8a[:, 0:1], -1.0))
            dv5 = p.op("vector", lambda E: E.tensor_scalar(
                maskb[:], s_sb[:], m8b[:, 7:8], None, ALU.is_ge), inc="dv")
            p.wait("scalar", "dv", dv5)
            ac_e = p.op("scalar", lambda E: E.activation(exps[:], s_sb[:], AF.Exp,
                                                         bias=c_negm1), inc="ac")
            p.wait("vector", "ac", ac_e)
            dv6 = p.op("vector", lambda E: E.scalar_tensor_tensor(
                gnum[:], exps[:], 1.0, maskb[:], ALU.mult, ALU.mult, accum_out=c_Z64),
                inc="dv")
            p.wait("vector", "dv", dv6)
            dv7 = p.op("vector", lambda E: E.reciprocal(c_rz64, c_Z64), inc="dv")
            p.wait("vector", "dv", dv7)
            p.op("vector", lambda E, tt=t: E.tensor_scalar(
                gate[:, tt * NE:(tt + 1) * NE], gnum[:], c_rz64, 2.0, ALU.mult, ALU.mult),
                inc="dv")
        dv_gate = p.cur("dv")
        p.release(("astream", 0), "pe", p.cur("pe"))
        p.release(("astream", 1), "pe", p.cur("pe"))
        zbanks = [("ps_a", 0), ("ps_a", 1), ("ps_c", 0), ("ps_c", 1)]
        zb_ap = [ps_a[0], ps_a[1], ps_c[0], ps_c[1]]
        for tg in range(2):
            for ti in range(4):
                p.need("tensor", zbanks[ti])
            for quarter in range(4):
                par = quarter % 2
                p.need("gpsimd", ("astream", par))
                gq = p.op("gpsimd", lambda E, qq=quarter, pp=par: E.dma_start(
                    out=aq_view[pp],
                    in_=dawt.rearrange("(q p) n -> p q n", p=128)[:, qq * 8:(qq + 1) * 8, :]),
                    inc="dga%d" % par, incby=16)
                p.wait("tensor", "dga%d" % par, gq)
                pe_zq = None
                for k8 in range(8):
                    k = quarter * 8 + k8
                    for ti in range(4):
                        t = tg * 4 + ti
                        pe_zq = p.op("tensor", lambda E, b=zb_ap[ti], kk=k, k8_=k8, pp=par,
                                     tt=t:
                             E.matmul(b[:], xnT[:, kk, tt * 128:(tt + 1) * 128],
                                      aq_view[pp][:, k8_, :],
                                      start=(kk == 0), stop=(kk == NK - 1)),
                             inc="pe" if (k8 == 7 and ti == 3) else None)
                p.release(("astream", par), "pe", pe_zq)
            pe_z = p.cur("pe")
            p.wait("vector", "pe", pe_z)
            p.wait("vector", "dv", dv_gate)
            for ti in range(4):
                t = tg * 4 + ti
                zpar = ti % 2
                p.need("vector", ("zgtok", zpar))
                gview = gate[:, t * NE:(t + 1) * NE].unsqueeze(2).broadcast_to([128, NE, RK])
                dv_zg = p.op("vector", lambda E, b=zb_ap[ti], zp=zpar, gv=gview:
                             E.tensor_tensor(
                                 zgtok[:, zp, :].rearrange("p (e r) -> p e r", r=RK),
                                 b[:].rearrange("p (e r) -> p e r", r=RK), gv, ALU.mult),
                             inc="dv")
                p.release(zbanks[ti], "dv", dv_zg)
                p.wait("tensor", "dv", dv_zg)
                tb = ti % 2
                p.need("tensor", ("ps_t", tb))
                pe_zt = None
                for j in range(4):
                    pe_zt = p.op("tensor", lambda E, b=tb, jj=j, zp=zpar: E.transpose(
                        ps_t[b][:, jj * 128:(jj + 1) * 128],
                        zgtok[:, zp, jj * 128:(jj + 1) * 128], ident[:]), inc="pe")
                p.release(("zgtok", zpar), "pe", pe_zt)
                p.wait("vector", "pe", pe_zt)
                p.need("vector", ("zgT_w",))
                dv_zt = p.op("vector", lambda E, b=tb, tt=t: E.tensor_copy(
                    zgT[:, :, tt * 128:(tt + 1) * 128],
                    ps_t[b][:, 0:512].rearrange("p (a b) -> p a b", b=128)), inc="dv")
                p.release(("ps_t", tb), "dv", dv_zt)

        for blk in range(4):
            p.release(("spb", blk), "pe", p.cur("pe"))
        for blk in range(8):
            p.release(("unb", blk), "pe", p.cur("pe"))

        qevac = {}

        def mk_qproj_dma(sp, half):
            def emit():
                p.need("gpsimd", ("wstream_h", half))
                v = p.op("gpsimd", lambda E, hh=half, ss=sp: E.dma_start(
                    out=wq_half_view[hh],
                    in_=dwq.rearrange("(k p) n -> p k n", p=128)
                        [:, hh * 16:(hh + 1) * 16, ss * 256:(ss + 1) * 256]),
                    inc="dgw%d" % half, incby=16)
                qdma[(sp, half)] = v
            return emit

        qproj_last_pe = [0]

        def mk_qproj_group(sp, m, tt):
            def emit():
                h = 2 * sp + m
                bank = tt
                p.need("tensor", ("ps_a", bank))
                pe_qp = None
                for k in range(NK):
                    half = k // 16
                    p.wait("tensor", "dgw%d" % half, qdma[(sp, half)])
                    pe_qp = p.op("tensor", lambda E, b=bank, kk=k, hh=half, mm=m, t2=tt:
                                 E.matmul(
                        ps_a[b][:], wq_half_view[hh][:, kk % 16, mm * 128:(mm + 1) * 128],
                        xnT[:, kk, t2 * 512:(t2 + 1) * 512],
                        start=(kk == 0), stop=(kk == NK - 1)),
                        inc="pe" if k == NK - 1 else None)
                qproj_last_pe[0] = pe_qp
                if m == 1 and tt == 1:
                    p.release(("wstream_h", 0), "pe", pe_qp)
                    p.release(("wstream_h", 1), "pe", pe_qp)
                p.wait("vector", "pe", pe_qp)
                p.need("vector", ("QC_w",))
                dv_qe = p.op("vector", lambda E, b=bank, hh=h, t2=tt: E.tensor_scalar_mul(
                    QC[:, hh, t2 * 512:(t2 + 1) * 512], ps_a[b][:], SCALE), inc="dv")
                p.release(("ps_a", bank), "dv", dv_qe)
                qevac[(h, tt)] = dv_qe
            return emit

        qdma = {}
        qpend = []
        qsup_done = -1
        for sp in range(16):
            qpend.append(("d", sp, mk_qproj_dma(sp, 0)))
            qpend.append(("d", sp, mk_qproj_dma(sp, 1)))
            for m in range(2):
                for tt in range(2):
                    qpend.append(("g", sp, mk_qproj_group(sp, m, tt)))

        def qpop(n=1):
            nonlocal_done = 0
            while qpend and nonlocal_done < n:
                kind, sp_, fn = qpend.pop(0)
                fn()
                if kind == "g":
                    nonlocal_done += 1

        def qdrain_through(sp):
            while qpend and qpend[0][1] <= sp:
                kind, sp_, fn = qpend.pop(0)
                fn()

        for sem in ("dsr0", "dsr1", "do0", "do1"):
            p.wait("sync", sem, p.cur(sem))
        p.need("sync", ("abbuf",))
        ab_ds = p.op("sync", lambda E: E.dma_start(out=v_ab, in_=dab[0]),
                     inc="dsab", incby=16)
        scnt = 0
        for h in range(H):
            qdrain_through(h // 2)
            p.wait("vector", "dsab", ab_ds)
            W = WIN[h]
            narrow = (W + 255) <= 1024
            if h > 0 and (narrow != ((WIN[h - 1] + 255) <= 1024)):
                while tv_pend:
                    emit_tv(tv_pend.pop(0))
            wh = 1024 if narrow else 2048
            nun = 8192 // wh
            nsp = 4096 // wh
            lag = 2 if narrow else 1
            qt_info = []
            for qt in range(NT):
                q0 = tok0 + qt * 128
                kb_lo = max(0, ((q0 - W) // 128) * 128) if q0 - W > 0 else 0
                qt_info.append((q0, kb_lo, q0 + 128 - kb_lo))
            last_tt_dv = None
            if h == 0:
                un_ready = {}
                un_view = {}
                tv_pend = []
                sB_loaded = False
            if h >= 15:
                if not sB_loaded:
                    sB_loaded = True
                    while tv_pend:
                        emit_tv(tv_pend.pop(0))
                    p.need("gpsimd", ("abbuf",))
                    for sl_ in (0, 1):
                        p.need("gpsimd", ("xrd", sl_))
                        p.need("gpsimd", ("ost", sl_))
                    for sem_ in ("dsab", "dsr0", "dsr1", "do0", "do1"):
                        p.wait("gpsimd", sem_, p.cur(sem_))
                    g_trx = p.op("gpsimd", lambda E: E.dma_start(out=v_trx, in_=dtrx),
                                 inc="dgp", incby=16)
                    g_bc = p.op("gpsimd", lambda E: E.dma_start(
                        out=v_bc, in_=dbc), inc="dgp", incby=16)
                    p.wait("vector", "dgp", p.cur("dgp"))
                    p.wait("scalar", "dgp", p.cur("dgp"))
                    p.wait("scalar", "dsab", p.cur("dsab"))
                    p.wait("tensor", "dv", dv_ones)
                    p.release(("xstage", 0), "ac", p.cur("ac"))
                    p.release(("xstage_dv", 0), "ac", p.cur("ac"))
                zr_f = v_zrow[0]
                zr_f2 = v_zrow[1]
                zr_b = scratch[0:1, 5568:5824].bitcast(BF16)
                zb_sb = astream[:, 2048:2560]
                for g2 in range(2):
                    qt0 = 4 * g2
                    q_g = tok0 + qt0 * 128
                    kb0 = qt_info[qt0][1]
                    jlo = kb0 // 128
                    jhi = (q_g + 512) // 128
                    avbank = g2 % 2
                    p.wait("tensor", "dv", qevac[(h, g2)])
                    first = True
                    pend = []
                    use4 = not qpend
                    sbanks = ([("ps_s", 0, ps_s[0]), ("ps_s", 1, ps_s[1]),
                               ("ps_a", 0, ps_a[0]), ("ps_a", 1, ps_a[1])]
                              if use4 else
                              [("ps_s", 0, ps_s[0]), ("ps_s", 1, ps_s[1])])
                    lag = 1
                    slot_blks = (0, 1, 3)

                    def emit_avz(item, jhi=jhi, avbank=avbank):
                        nonlocal first
                        pj, pblk, pac = item
                        p.wait("tensor", "ac", pac)
                        if first:
                            p.need("tensor", ("ps_c", avbank))
                            p.need("tensor", ("ps_t", 0))
                        pe_av = p.op("tensor", lambda E, b=avbank, jj_=pj, ss=pblk,
                                     st=first, fin=(pj == jhi - 1): E.matmul(
                            ps_c[b][:, 0:512], vT[:, jj_ * 128:(jj_ + 1) * 128],
                            astream[:, ss * 1024:ss * 1024 + 512],
                            start=st, stop=fin), inc="pe")
                        pe_zl = p.op("tensor", lambda E, ss=pblk, st=first,
                                     fin=(pj == jhi - 1): E.matmul(
                            ps_tf[0][0:1, 0:512], onescol[:],
                            astream[:, ss * 1024:ss * 1024 + 512],
                            start=st, stop=fin), inc="pe")
                        p.release(("unb", pblk), "pe", pe_zl)
                        first = False
                        return pe_zl

                    for j in range(jlo, jhi):
                        bkey, bidx, bap = sbanks[scnt % len(sbanks)]
                        scnt += 1
                        p.need("tensor", (bkey, bidx))
                        pe_sc = p.op("tensor", lambda E, b=bap, hh=h, jj_=j,
                                     qq=qt0: E.matmul(
                            b[:, 0:512], kT[:, jj_ * 128:(jj_ + 1) * 128],
                            QC[:, hh, qq * 128:qq * 128 + 512],
                            start=True, stop=True), inc="pe")
                        gate_sem, gate_val = "pe", pe_sc
                        jj = j - q_g // 128
                        if jj >= 0:
                            off = 384 - 128 * jj
                            p.wait("vector", "pe", pe_sc)
                            gate_val = p.op("vector", lambda E, b=bap, oo=off:
                                            E.tensor_tensor(
                                b[:, 0:512], b[:, 0:512],
                                v_trx[:, oo:oo + 512], ALU.add), inc="dv")
                            gate_sem = "dv"
                        blk = slot_blks[j % 3]
                        p.wait("scalar", gate_sem, gate_val)
                        p.need("scalar", ("unb", blk))
                        idx = (j - 8 * c - qt0 - 3) + 17
                        ac_exp = p.op("scalar", lambda E, b=bap, ss=blk, hh=h,
                                      ii=idx: E.activation(
                            astream[:, ss * 1024:ss * 1024 + 512], b[:, 0:512],
                            AF.Exp, bias=v_bc[:, hh, ii:ii + 1]), inc="ac")
                        p.release((bkey, bidx), "ac", ac_exp)
                        pend.append((j, blk, ac_exp))
                        while len(pend) > lag:
                            pe_z = emit_avz(pend.pop(0))
                    while pend:
                        pe_z = emit_avz(pend.pop(0))
                    p.wait("vector", "pe", pe_z)
                    dv_zr = p.op("vector", lambda E: E.tensor_copy(
                        zr_f[:, 0:512], ps_tf[0][0:1, 0:512]), inc="dv")
                    p.release(("ps_t", 0), "dv", dv_zr)
                    p.wait("vector", "dv", dv_zr)
                    dv_rc = p.op("vector", lambda E: E.reciprocal(zr_f2[:, 0:512],
                                                                  zr_f[:, 0:512]),
                                 inc="dv")
                    p.wait("vector", "dv", dv_rc)
                    dv_rc = p.op("vector", lambda E: E.tensor_copy(zr_b, zr_f2[:, 0:512]),
                                 inc="dv")
                    p.wait("tensor", "dv", dv_rc)
                    p.need("tensor", ("ps_t", 1))
                    pe_bc = p.op("tensor", lambda E: E.matmul(
                        ps_tf[1][:, 0:512], onesr[:], zr_b, start=True, stop=True),
                        inc="pe")
                    p.wait("scalar", "pe", pe_bc)
                    p.need("scalar", ("unb", 2))
                    dv_zb = p.op("scalar", lambda E: E.copy(zb_sb, ps_tf[1][:]),
                                 inc="ac")
                    p.release(("ps_t", 1), "ac", dv_zb)
                    p.wait("vector", "ac", dv_zb)
                    p.need("vector", ("QC_w",))
                    dv_ctx = p.op("vector", lambda E, b=avbank, hh=h, gg=g2:
                                  E.tensor_tensor(
                        QC[:, hh, gg * 512:(gg + 1) * 512], ps_c[b][:, 0:512],
                        zb_sb, ALU.mult), inc="dv")
                    p.release(("ps_c", avbank), "dv", dv_ctx)
                    p.release(("unb", 2), "dv", dv_ctx)
                    if g2 == 0:
                        qpop(1)
                continue

            def emit_tv(args):
                hh_, g_, qt_info_ = args
                q00, kb0, _ = qt_info_[2 * g_]
                q01, kb1, _ = qt_info_[2 * g_ + 1]
                jlo = min(kb0, kb1) // 128
                jhi = (q01 + 128) // 128
                avbank = g_ % 2
                first_av = True
                pe_av_last = None
                pend_av = None
                pairs = [list(range(j, min(j + 2, jhi))) for j in range(jlo, jhi, 2)]
                last_tp_qt = {}
                for pj in pairs + [None]:
                    if pj is not None:
                        tb = (pj[0] // 2) % 2
                        p.need("tensor", ("ps_t", tb))
                        pe_tp = None
                        ops = []
                        for jr, j in enumerate(pj):
                            for qi in (0, 1):
                                qt = 2 * g_ + qi
                                if not (qt_info_[qt][1] // 128 <= j
                                        <= qt_info_[qt][0] // 128):
                                    continue
                                off = j * 128 - qt_info_[qt][1]
                                ops.append((jr, qi, qt, off))
                        for on, (jr, qi, qt, off) in enumerate(ops):
                            p.wait("tensor", "ac", un_ready[(hh_, qt)])
                            pe_tp = p.op("tensor", lambda E, b=tb, col=jr * 256 + qi * 128,
                                         uv=un_view[(hh_, qt)], oo=off: E.transpose(
                                ps_t[b][:, col:col + 128],
                                uv[:, oo:oo + 128], ident[:]), inc="pe")
                            last_tp_qt[qt] = pe_tp
                        ev_eng = "vector" if tb == 0 else "scalar"
                        ev_sem = "dv" if tb == 0 else "ac"
                        p.wait(ev_eng, "pe", pe_tp)
                        p.need(ev_eng, ("ut", tb))
                        if ev_eng == "vector":
                            cpf = lambda E, o, i: E.tensor_copy(o, i)
                            msf = lambda E, o: E.memset(o, 0.0)
                        else:
                            cpf = lambda E, o, i: E.copy(o, i)
                            msf = lambda E, o: E.memzero(o)
                        wid = len(pj) * 256
                        ev = p.op(ev_eng, lambda E, b=tb, u=v_ut[tb], w=wid, f=cpf:
                                  f(E, u[:, 0:w], ps_t[b][:, 0:w]), inc=ev_sem)
                        have = {(jr, qi) for (jr, qi, qt, off) in ops}
                        for jr, j in enumerate(pj):
                            for qi in (0, 1):
                                if (jr, qi) not in have:
                                    p.wait(ev_eng, ev_sem, ev)
                                    ev = p.op(ev_eng, lambda E, u=v_ut[tb],
                                              col=jr * 256 + qi * 128, f=msf: f(
                                        E, u[:, col:col + 128]), inc=ev_sem)
                        p.release(("ps_t", tb), ev_sem, ev)
                    if pend_av is not None:
                        pj_, tb_, es_, evv_ = pend_av
                        p.wait("tensor", es_, evv_)
                        if first_av:
                            p.need("tensor", ("ps_c", avbank))
                        for jr, j in enumerate(pj_):
                            pe_av_last = p.op("tensor", lambda E, b=avbank, jj=j,
                                              u=v_ut[tb_], c0=jr * 256, st=first_av,
                                              fin=(jj == jhi - 1) if False else (j == jhi - 1):
                                              E.matmul(
                                ps_c[b][:, 0:256], vT[:, jj * 128:(jj + 1) * 128],
                                u[:, c0:c0 + 256], start=st, stop=fin), inc="pe")
                            first_av = False
                        p.release(("ut", tb_), "pe", pe_av_last)
                    if pj is not None:
                        pend_av = (pj, tb, ev_sem, ev)
                p.wait("scalar", "pe", pe_av_last)
                ac_ctx = p.op("scalar", lambda E, b=avbank, hh=hh_, gg=g_: E.copy(
                    QC[:, hh, gg * 256:(gg + 1) * 256], ps_c[b][:, 0:256]), inc="ac")
                p.release(("ps_c", avbank), "ac", ac_ctx)
                for qtr in (2 * g_, 2 * g_ + 1):
                    uv = un_view[(hh_, qtr)]
                    b0 = uv.offset // 1024
                    nb = uv.shape[-1] // 1024
                    for blk in range(b0, b0 + nb):
                        p.release(("unb", blk), "pe", pe_av_last)

            for g in range(4):
                if g < 3:
                    qpop(1)
                for qi in range(2):
                    qt = 2 * g + qi
                    q0, kb_lo, L = qt_info[qt]
                    nch = (L + 511) // 512
                    spar = qt % nsp
                    us = qt % nun
                    vsp = scratch[:, spar * wh:spar * wh + wh]
                    vun = astream[:, us * wh:us * wh + wh]
                    un_view[(h, qt)] = vun
                    for blk in range(spar * wh // 1024, (spar + 1) * wh // 1024):
                        p.need("vector", ("spb", blk))
                    p.wait("tensor", "dv", qevac[(h, qt // 4)])
                    for ci in range(nch):
                        colw = min(512, L - ci * 512)
                        bank = scnt % 2
                        scnt += 1
                        p.need("tensor", ("ps_s", bank))
                        pe_sc = p.op("tensor", lambda E, b=bank, hh=h, qq=qt, k0=kb_lo,
                                     cc=ci, w=colw: E.matmul(
                            ps_s[b][:, 0:w], QC[:, hh, qq * 128:(qq + 1) * 128],
                            kT[:, k0 + cc * 512: k0 + cc * 512 + w],
                            start=True, stop=True), inc="pe")
                        p.wait("vector", "pe", pe_sc)
                        aboff = kb_lo - q0 + S + ci * 512
                        dv_tt = p.op("vector", lambda E, b=bank, cc=ci, w=colw, vv=vsp,
                                     ao=aboff: E.tensor_tensor(
                                         vv[:, cc * 512:cc * 512 + w],
                                         ps_s[b][:, 0:w], v_ab[:, ao:ao + w], ALU.add),
                                     inc="dv")
                        p.release(("ps_s", bank), "dv", dv_tt)
                        last_tt_dv = dv_tt
                    p.wait("scalar", "dv", last_tt_dv)
                    for blk in range(us * wh // 1024, (us + 1) * wh // 1024):
                        p.need("scalar", ("unb", blk))
                    ac_exp = p.op("scalar", lambda E, uv=vun, vv=vsp, ss=us, LL=L:
                                  E.activation(
                        uv[:, 0:LL], vv[:, 0:LL], AF.Exp,
                        bias=c_negB, accum_out=c_Z[ss]), inc="ac")
                    for blk in range(spar * wh // 1024, (spar + 1) * wh // 1024):
                        p.release(("spb", blk), "ac", ac_exp)
                    p.wait("vector", "ac", ac_exp)
                    dv_rz = p.op("vector", lambda E, ss=us: E.reciprocal(
                        c_rz[ss], c_Z[ss]), inc="dv")
                    p.wait("scalar", "dv", dv_rz)
                    un_ready[(h, qt)] = p.op("scalar", lambda E, uv=vun, ss=us, LL=L:
                                             E.activation(
                        uv[:, 0:LL], uv[:, 0:LL], AF.Identity, scale=c_rz[ss]), inc="ac")
                tv_pend.append((h, g, qt_info))
                if len(tv_pend) > lag:
                    emit_tv(tv_pend.pop(0))
            if h == H - 1:
                while tv_pend:
                    emit_tv(tv_pend.pop(0))
            nxt = h + 1 if h + 1 <= 14 else None
            p.release(("abbuf",), "dv", last_tt_dv)
            if nxt is not None:
                p.need("sync", ("abbuf",))
                ab_ds = p.op("sync", lambda E, hh=nxt: E.dma_start(out=v_ab, in_=dab[hh]),
                             inc="dsab", incby=16)
        while qpend:
            kind, sp_, fn = qpend.pop(0)
            fn()
        p.release(("wo", 0), "pe", qproj_last_pe[0])
        p.release(("wo", 1), "pe", qproj_last_pe[0])

        ac_attn_end = p.cur("ac")
        dv_attn_end = p.cur("dv")
        for sl in (0, 1):
            p.release(("xrd", sl), "dv", dv_attn_end)
            p.release(("ost", sl), "dv", dv_attn_end)
        p.release(("xnstage",), "pe", p.cur("pe"))
        p.release(("astream", 0), "pe", p.cur("pe"))
        p.release(("astream", 1), "pe", p.cur("pe"))
        p.release(("abbuf",), "dv", dv_attn_end)

        p.wait("tensor", "ac", ac_attn_end)
        p.wait("sync", "dsab", p.cur("dsab"))
        p.wait("sync", "ac", p.cur("ac"))
        p.wait("sync", "dgp", p.cur("dgp"))
        p.wait("vector", "dsab", p.cur("dsab"))
        p.wait("scalar", "dsab", p.cur("dsab"))
        pe_fin_last = None
        for fsp in range(8):
            par = fsp % 2
            p.need("gpsimd", ("wo", par))
            g_wo = None
            for quarter in range(4):
                g_wo = p.op("gpsimd", lambda E, qq=quarter, pp=par, ff=fsp: E.dma_start(
                    out=wo_view[pp][:, qq * 8:(qq + 1) * 8, :],
                    in_=dwo.rearrange("(q p) n -> p q n", p=128)
                        [:, qq * 8:(qq + 1) * 8, ff * 512:(ff + 1) * 512]),
                    inc="dgo%d" % par, incby=16)
            p.need("gpsimd", ("wstream_h", par))
            g_b = p.op("gpsimd", lambda E, pp=par, ff=fsp: E.dma_start(
                out=b_view[pp],
                in_=dbwt.rearrange("(k p) n -> p k n", p=128)[:, :, ff * 512:(ff + 1) * 512]),
                inc="dgw%d" % par, incby=16)
            for t in range(NT):
                row0 = tok0 + t * 128
                sl = t % 2
                p.need("sync", ("xrd", sl))
                ds_x = p.op("sync", lambda E, r=row0, ss=sl, ff=fsp: E.dma_start(
                    out=v_xrd[ss], in_=dx[r:r + 128, ff * 512:(ff + 1) * 512]),
                    inc="dsr%d" % sl, incby=16)
                bank = t % 2
                p.need("tensor", ("ps_a", bank))
                p.wait("tensor", "dgo%d" % par, g_wo)
                p.wait("tensor", "dgw%d" % par, g_b)
                for k in range(NK):
                    p.op("tensor", lambda E, b=bank, kk=k, pp=par, tt=t: E.matmul(
                        ps_a[b][:], QC[:, kk, tt * 128:(tt + 1) * 128], wo_view[pp][:, kk, :],
                        start=(kk == 0), stop=False))
                pe_f = None
                for n in range(4):
                    pe_f = p.op("tensor", lambda E, b=bank, nn=n, pp=par, tt=t: E.matmul(
                        ps_a[b][:], zgT[:, nn, tt * 128:(tt + 1) * 128], b_view[pp][:, nn, :],
                        start=False, stop=(nn == 3)),
                        inc="pe" if n == 3 else None)
                pe_fin_last = pe_f
                if t == NT - 1:
                    p.release(("wo", par), "pe", pe_f)
                    p.release(("wstream_h", par), "pe", pe_f)
                p.wait("vector", "pe", pe_f)
                p.wait("vector", "dsr%d" % sl, ds_x)
                p.need("vector", ("ost", sl))
                dv_o = p.op("vector", lambda E, b=bank, ss=sl: E.tensor_add(
                    v_ost[ss], ps_a[b][:], v_xrd[ss]), inc="dv")
                p.release(("ps_a", bank), "dv", dv_o)
                p.release(("xrd", sl), "dv", dv_o)
                p.wait("scalar", "dv", dv_o)
                do_v = p.op("scalar", lambda E, r=row0, ss=sl, ff=fsp: E.dma_start(
                    out=dout[r:r + 128, ff * 512:(ff + 1) * 512], in_=v_ost[ss]),
                    inc="do%d" % sl, incby=16)
                p.release(("ost", sl), "do%d" % sl, do_v)
        p.release(("xnT_w",), "pe", pe_fin_last)
        p.release(("zgT_w",), "pe", pe_fin_last)
        p.release(("QC_w",), "pe", pe_fin_last)


    p.wait("scalar", "do0", p.cur("do0"))
    p.wait("scalar", "do1", p.cur("do1"))

    with ctx:
        with nc.Block() as block:
            block.sync(lambda E: [f(E) for f in p.q["sync"]])
            block.gpsimd(lambda E: [f(E) for f in p.q["gpsimd"]])
            block.tensor(lambda E: [f(E) for f in p.q["tensor"]])
            block.vector(lambda E: [f(E) for f in p.q["vector"]])
            block.scalar(lambda E: [f(E) for f in p.q["scalar"]])
    return nc


_CACHED = {}


def _host_constants():
    if "ab" not in _CACHED:
        krel = np.arange(-S, 128, dtype=np.float32)
        i = np.arange(128, dtype=np.float32)
        diff = krel[None, :] - i[:, None]
        ab = np.empty((H, 128, S + 128), dtype=np.float32)
        for h in range(H):
            ab[h] = np.where(diff <= 0, SLOPES[h] * diff, NEG)
        _CACHED["ab"] = ab
        _CACHED["ident"] = np.eye(128, dtype=np.float32)
        x_ = np.arange(128, dtype=np.float32)
        bc = np.empty((128, H, 18), dtype=np.float32)
        for h in range(H):
            for idx in range(18):
                m = idx - 17
                bc[:, h, idx] = SLOPES[h] * (x_ + 128.0 * m - 127.0) - 30.0
        _CACHED["bc"] = bc
        z_ = np.arange(896, dtype=np.float32)
        _CACHED["trx"] = np.where(x_[:, None] > z_[None, :] - 384.0, NEG, 0.0
                                  ).astype(np.float32)
    return _CACHED


def kernel(L0_attn_hidden_states, L0_attn_Wq, L0_attn_Wkv, L0_attn_Wo,
           L0_ffn_A_w, L0_ffn_B_w, L0_ffn_Wq_w, L0_ffn_keys):
    x = np.ascontiguousarray(np.asarray(L0_attn_hidden_states, dtype=np.float32))
    cst = _host_constants()
    shared = {
        "wq": np.ascontiguousarray(np.asarray(L0_attn_Wq, dtype=np.float32)),
        "wkv": np.ascontiguousarray(np.asarray(L0_attn_Wkv, dtype=np.float32)),
        "wo": np.ascontiguousarray(np.asarray(L0_attn_Wo, dtype=np.float32)),
        "awt": np.ascontiguousarray(np.asarray(L0_ffn_A_w, dtype=np.float32).T),
        "bwt": np.ascontiguousarray(np.asarray(L0_ffn_B_w, dtype=np.float32).T),
        "wqwt": np.ascontiguousarray(np.asarray(L0_ffn_Wq_w, dtype=np.float32).T),
        "keyst": np.ascontiguousarray(np.asarray(L0_ffn_keys, dtype=np.float32).T),
        "ab": cst["ab"], "ident": cst["ident"], "bc": cst["bc"], "trx": cst["trx"],
    }
    nc = build_program()
    in_maps = [dict(shared, x=np.ascontiguousarray(x[i])) for i in range(8)]
    res = run_bass_kernel_spmd(nc, in_maps, core_ids=list(range(8)))
    return np.stack([r["out"] for r in res.results], axis=0)
